# version 03
# baseline (speedup 1.0000x reference)
import math
from contextlib import ExitStack

import numpy as np
import ml_dtypes

import concourse.bass as bass
import concourse.mybir as mybir
from concourse.bass_utils import run_bass_kernel_spmd

F32 = mybir.dt.float32
BF16 = mybir.dt.bfloat16
U8 = mybir.dt.uint8
AF = mybir.ActivationFunctionType
ALU = mybir.AluOpType
AX = mybir.AxisListType

E = 2048
ECH = 16
HEADS = 16
G = 4
D = 128
DKV = 512
HID = 5632
NJ = 44
SWA = 1024
OWN = 2048
CT = 1024
NCHUNK = 2
SCALE = 1.0 / math.sqrt(128.0)
NEG = -1e30
EPS = 1e-5

JH = 22
NPIECE = 2


class _MaybeBlock:
    def __init__(self, nc, on):
        self.on = on
        self.blk = nc.Block() if on else None
    def __enter__(self):
        if self.on:
            return self.blk.__enter__()
        return _NullBlk()
    def __exit__(self, *a):
        if self.on:
            return self.blk.__exit__(*a)
        return False


class _NullBlk:
    def sync(self, f): pass
    def gpsimd(self, f): pass
    def tensor(self, f): pass
    def vector(self, f): pass
    def scalar(self, f): pass


def build_nc(nphase=999, dbg=None, slow=False):
    _ph = [0]

    def phase_on():
        _ph[0] += 1
        return _ph[0] <= nphase

    nc = bass.Bass("TRN2", target_bir_lowering=False)

    x3 = nc.dram_tensor("x3", [18432, E], F32, kind="ExternalInput")
    xloc = nc.dram_tensor("xloc", [3072, E], F32, kind="Internal")
    wq = nc.dram_tensor("wq", [E, E], F32, kind="ExternalInput")
    wk = nc.dram_tensor("wk", [E, DKV], F32, kind="ExternalInput")
    wv = nc.dram_tensor("wv", [E, DKV], F32, kind="ExternalInput")
    wo = nc.dram_tensor("wo", [E, E], F32, kind="ExternalInput")
    bo_col = nc.dram_tensor("bo_col", [128, 16], F32, kind="ExternalInput")
    g_blk = nc.dram_tensor("g_blk", [NJ, ECH, 128, 128], U8, kind="ExternalInput")
    u_blk = nc.dram_tensor("u_blk", [NJ, ECH, 128, 128], U8, kind="ExternalInput")
    d_blk = nc.dram_tensor("d_blk", [4, 4, 128, 11, 512], U8, kind="ExternalInput")
    sg_col = nc.dram_tensor("sg_col", [128, NJ], F32, kind="ExternalInput")
    su_col = nc.dram_tensor("su_col", [128, NJ], F32, kind="ExternalInput")
    sd_col = nc.dram_tensor("sd_col", [128, 16], F32, kind="ExternalInput")
    trim0 = nc.dram_tensor("trim0", [128, 128], F32, kind="ExternalInput")
    trim8 = nc.dram_tensor("trim8", [128, 128], F32, kind="ExternalInput")
    hbias = nc.dram_tensor("hbias", [1024], BF16, kind="ExternalInput")
    out = nc.dram_tensor("out", [OWN, E], F32, kind="ExternalOutput")

    ctx = ExitStack()

    def sb(name, shape, dtype):
        return ctx.enter_context(nc.sbuf_tensor(name, shape, dtype))

    big = sb("big", [128, 20480], F32)
    xn_own = sb("xn_own", [128, ECH, 1024], BF16)
    att = sb("att", [128, 16384], BF16)
    hbuf = sb("hbuf", [128, 22528], BF16)

    ident_bf = sb("ident_bf", [128, 128], BF16)
    ident_f = sb("ident_f", [128, 128], F32)
    trim0_sb = sb("trim0_sb", [128, 128], F32)
    trim8_sb = sb("trim8_sb", [128, 128], F32)
    hbias_sb = sb("hbias_sb", [1, 1024], BF16)
    sg_sb = sb("sg_sb", [128, NJ], F32)
    su_sb = sb("su_sb", [128, NJ], F32)
    sd_sb = sb("sd_sb", [128, 16], F32)
    bo_sb = sb("bo_sb", [128, 16], F32)
    onecol = sb("onecol", [128, 1], BF16)
    onerow = sb("onerow", [1, 128], BF16)
    neg128row = sb("neg128row", [1, 128], BF16)
    stats = sb("stats", [128, 160], F32)
    asum = sb("asum", [128, 8], F32)
    smv = sb("smv", [128, 16], F32)
    colsum_bf = sb("colsum_bf", [1, 1024], BF16)
    hsum32 = sb("hsum32", [1, 1024], F32)
    hsum_hi = sb("hsum_hi", [1, 1024], BF16)
    eps_col = sb("eps_col", [128, 1], F32)
    flushc = sb("flushc", [128, 4], F32)
    flushw = sb("flushw", [128, 512], F32)
    flushb = sb("flushb", [128, 128], BF16)
    hsum_lo = sb("hsum_lo", [1, 1024], BF16)

    pb = [ctx.enter_context(nc.psum_tensor(f"pb{i}", [128, 512], F32))
          for i in range(8)]

    def bank(b, lo=0, hi=512):
        return pb[b][:, lo:hi]

    def bank_bf(b, lo=0, hi=1024):
        return pb[b][:, :].bitcast(BF16)[:, lo:hi]

    def big_bf(lo_f32, n_bf16):
        return big[:, lo_f32: lo_f32 + (n_bf16 + 1) // 2].bitcast(BF16)[:, 0:n_bf16]

    def xt_v(b):
        return big[:, b * 2048:(b + 1) * 2048]

    def sq_v():
        return big[:, 4096:6144]

    def xn_halo_e(e):
        return big_bf(8192 + e * 512, 1024)

    def q_v():
        return big_bf(0, 1024)

    def p_v(b):
        return big_bf(512 + b * 576, 1152)

    def pT_v():
        return big_bf(1664, 1152)

    def wq_v(b):
        return hbuf[:, 16384 + b * 2048: 16384 + (b + 1) * 2048]

    def oa_v(m2c):
        return big[:, m2c * 1024:(m2c + 1) * 1024]

    def xres_v():
        return big[:, 16384:18432]

    def osb_v():
        return big[:, 18432:20480]

    def k_e(gidx):
        return hbuf[:, gidx * 2048:(gidx + 1) * 2048]

    def v_st(st):
        return hbuf[:, 8192 + st * 512: 8192 + (st + 1) * 512]

    def wo_v(b):
        return hbuf[:, b * 512:(b + 1) * 512]

    def h_v(j):
        return hbuf[:, j * 1024:(j + 1) * 1024]

    def wg_v(b):
        return att[:, b * 2048:(b + 1) * 2048]

    def wu_v(b):
        return att[:, 4096 + b * 2048: 4096 + (b + 1) * 2048]

    def g_v():
        return att[:, 8192:9216]

    def u_v():
        return att[:, 9216:10240]

    def wd_v(b):
        return att[:, b * 5632:(b + 1) * 5632]

    def att_v(m):
        return att[:, m * 1024:(m + 1) * 1024]

    def xn_tm_v(i):
        return att[:, i * 2048:(i + 1) * 2048]

    def xn_all_e(e, lo, hi):
        assert (lo >= 1024) or (hi <= 1024)
        if hi <= 1024:
            return xn_halo_e(e)[:, lo:hi]
        return xn_own[:, e, lo - 1024:hi - 1024]

    _xoff = {"sv": None, "rel": 0}

    def xoff_to(sync, rel):
        if _xoff["sv"] is None:
            pid = sync.partition_id()
            _xoff["sv"] = sync.snap(pid * 2048 + (pid // 4) * 1024 + rel)
        elif rel != _xoff["rel"]:
            _xoff["sv"] = sync.snap(_xoff["sv"] + (rel - _xoff["rel"]), donate=True)
        _xoff["rel"] = rel
        return _xoff["sv"]

    sems = {}

    class S:
        def __init__(self, name):
            self.h = ctx.enter_context(nc.semaphore(name))
            self.v = 0

    for nm in ["ldsm", "ldx", "xnfree", "dv1", "ac1", "dv2", "ac2", "dv3",
               "xnrdy", "tpd", "tpe",
               "wkv", "kmm", "kev",
               "wql", "qmm", "qev", "scd", "msk", "exd", "pnd", "ptd", "pte",
               "pvd", "atv",
               "wol", "c4m", "c4e", "csd",
               "gul", "c3m", "c3sil", "c3up", "c3h", "hsm", "hse", "hsd",
               "wdl", "c5m", "c5e",
               "ldr", "tp6", "ev6", "sto"]:
        sems[nm] = S(nm)

    sm = {k: v for k, v in sems.items()}

    with nc.Block() as blk:
        @blk.sync
        def _(sync):
            sync.dma_start(trim0_sb[:, :], trim0.ap()).then_inc(sm["ldsm"].h, 16)
            sync.dma_start(trim8_sb[:, :], trim8.ap()).then_inc(sm["ldsm"].h, 16)
            sync.dma_start(sg_sb[:, :], sg_col.ap()).then_inc(sm["ldsm"].h, 16)
            sync.dma_start(su_sb[:, :], su_col.ap()).then_inc(sm["ldsm"].h, 16)
            sync.dma_start(sd_sb[:, :], sd_col.ap()).then_inc(sm["ldsm"].h, 16)
            sync.dma_start(bo_sb[:, :], bo_col.ap()).then_inc(sm["ldsm"].h, 16)
            sync.dma_start(hbias_sb[0:1, :], hbias.ap()[None, :]).then_inc(sm["ldsm"].h, 16)
            pid = sync.partition_id()
            off = sync.snap(pid * 2048 + (pid // 4) * 1024)
            for q in range(4):
                sync.dma_start(xloc.ap()[q * 768:(q + 1) * 768, :],
                               x3.ap()[bass.ds(off, 768), :]
                               ).then_inc(sm["ldsm"].h, 16)
                if q < 3:
                    off = sync.snap(off + 768, donate=True)
            sync.wait_ge(sm["ldsm"].h, 16 * 11)

        @blk.gpsimd
        def _(g):
            g.memset(ident_bf[:, :], 0.0)
            g.affine_select(out=ident_bf[:, :], in_=ident_bf[:, :],
                            compare_op=ALU.not_equal, fill=1.0, base=0,
                            pattern=[[-1, 128]], channel_multiplier=1)
            g.memset(ident_f[:, :], 0.0)
            g.affine_select(out=ident_f[:, :], in_=ident_f[:, :],
                            compare_op=ALU.not_equal, fill=1.0, base=0,
                            pattern=[[-1, 128]], channel_multiplier=1)
            g.memset(onecol[:, :], 1.0)
            g.memset(onerow[0:1, :], 1.0)
            g.memset(neg128row[0:1, :], -128.0)
            g.memset(eps_col[:, :], EPS)

    sm["ldsm"].v = 16 * 11

    if dbg == "echo":
        with nc.Block() as blk:
            @blk.sync
            def _(sync):
                for i in range(16):
                    sync.dma_start(big[:, 0:2048],
                                   xloc.ap()[i * 192:i * 192 + 128, :]
                                   ).then_inc(sm["ldx"].h, 16)
                    sync.wait_ge(sm["ldx"].h, 16 * (i + 1))
                    sync.dma_start(out.ap()[i * 128:(i + 1) * 128, :],
                                   big[:, 0:2048]).then_inc(sm["sto"].h, 16)
                    sync.wait_ge(sm["sto"].h, 16 * (i + 1))
                sync.dma_start(big[0:4, 0:2048], wq.ap()[0:4, :]).then_inc(sm["ldx"].h, 16)
                sync.dma_start(big[4:8, 0:2048], wq.ap()[2000:2004, :]).then_inc(sm["ldx"].h, 16)
                sync.dma_start(big[8:12, 0:2048], wo.ap()[1000:1004, :]).then_inc(sm["ldx"].h, 16)
                sync.dma_start(big[12:13, 0:2048].bitcast(U8)[:, 0:8192],
                               g_blk.ap()[40].rearrange("c p d -> (c p d)")[None, 0:8192]
                               ).then_inc(sm["ldx"].h, 16)
                sync.wait_ge(sm["ldx"].h, 256 + 64)
                sync.dma_start(out.ap()[2032:2048, :], big[0:16, 0:2048]).then_inc(sm["sto"].h, 16)
                sync.wait_ge(sm["sto"].h, 16 * 17)
        sm["ldx"].v += 256 + 64
        sm["sto"].v += 256 + 16

    for ch in range(NCHUNK):
        key0 = ch * 1024

        def _dump_bf16(get_ap, nrow_tiles, width):
            with nc.Block() as blk:
                @blk.vector
                def _(v):
                    pass
                @blk.sync
                def _(sync):
                    pass
            for i in range(nrow_tiles):
                with nc.Block() as blk:
                    @blk.vector
                    def _(v):
                        v.tensor_copy(big[:, 16384:16384 + width], get_ap(i))
                    @blk.sync
                    def _(sync):
                        pass
                with nc.Block() as blk:
                    @blk.sync
                    def _(sync):
                        sync.dma_start(out.ap()[i * 128:(i + 1) * 128, 0:width],
                                       big[:, 16384:16384 + width]).then_inc(sm["sto"].h, 16)
                        sync.wait_ge(sm["sto"].h, sm["sto"].v + 16 * (i + 1))
            sm["sto"].v += 16 * nrow_tiles


        base_ldx = sm["ldx"].v
        base_dv1, base_ac1 = sm["dv1"].v, sm["ac1"].v
        base_dv2, base_ac2 = sm["dv2"].v, sm["ac2"].v
        base_dv3 = sm["dv3"].v
        base_xnrdy = sm["xnrdy"].v
        base_tpd, base_tpe = sm["tpd"].v, sm["tpe"].v

        with _MaybeBlock(nc, phase_on()) as blk:
            @blk.sync
            def _(sync):
                for tt in range(16):
                    if tt >= 2:
                        sync.wait_ge(sm["xnrdy"].h, base_xnrdy + tt - 1)
                    sync.dma_start(xt_v(tt % 2)[:, :],
                                   xloc.ap()[key0 + tt * 128: key0 + (tt + 1) * 128, :]
                                   ).then_inc(sm["ldx"].h, 16)

            @blk.vector
            def _(v):
                c = lambda b, k: stats[:, b * 10 + k: b * 10 + k + 1]
                for gidx in range(4):
                    for ti in range(4):
                        tt = gidx * 4 + ti
                        b = tt
                        sp = lambda: v.tensor_copy(flushc[:, 3:4], eps_col[:, 0:1])
                        v.wait_ge(sm["ldx"].h, base_ldx + 16 * (tt + 1))
                        v.reduce_sum(c(b, 0), xt_v(tt % 2)[:, :], AX.X)
                        sp()
                        v.wait_ge(sm["ac1"].h, base_ac1 + tt + 1)
                        sp()
                        v.tensor_scalar(c(b, 2), c(b, 0), 1.0 / 2048.0, None, ALU.mult)
                        v.tensor_scalar(c(b, 3), c(b, 1), 1.0 / 2048.0, None, ALU.mult)
                        v.tensor_tensor(c(b, 4), c(b, 2), c(b, 2), ALU.mult)
                        sp()
                        v.tensor_tensor(c(b, 4), c(b, 3), c(b, 4), ALU.subtract)
                        v.tensor_copy(flushc[:, 0:1], c(b, 4)
                                      ).then_inc(sm["dv2"].h, 1)
                        v.wait_ge(sm["ac2"].h, base_ac2 + tt + 1)
                        v.tensor_copy(c(b, 8), c(b, 6))
                        v.tensor_tensor(c(b, 7), c(b, 2), c(b, 6), ALU.mult)
                        sp()
                        v.tensor_scalar(c(b, 7), c(b, 7), -1.0, None, ALU.mult)
                        sp()
                        v.tensor_copy(flushc[:, 0:1], c(b, 7))
                        v.tensor_copy(flushc[:, 1:2], c(b, 8))
                        v.tensor_copy(flushw[:, :], xt_v(tt % 2)[:, 0:512]
                                      ).then_inc(sm["dv3"].h, 1)
                    tt0 = gidx * 4
                    for e in range(ECH):
                        k = gidx * ECH + e
                        v.wait_ge(sm["tpd"].h, base_tpd + k + 1)
                        if tt0 < 8:
                            dst = xn_halo_e(e)[:, tt0 * 128:(tt0 + 4) * 128]
                        else:
                            dst = xn_own[:, e, (tt0 - 8) * 128:(tt0 - 4) * 128]
                        v.tensor_copy(dst, bank_bf(k % 2, 0, 512)
                                      ).then_inc(sm["tpe"].h, 1)

            @blk.scalar
            def _(s):
                c = lambda b, k: stats[:, b * 10 + k: b * 10 + k + 1]
                for tt in range(16):
                    b = tt
                    s.wait_ge(sm["ldx"].h, base_ldx + 16 * (tt + 1))
                    s.activation(sq_v()[:, :], xt_v(tt % 2)[:, :], AF.Square,
                                 accum_out=c(b, 1))
                    s.activation(flushc[:, 2:3], c(b, 1), AF.Copy
                                 ).then_inc(sm["ac1"].h, 1)
                    s.wait_ge(sm["dv2"].h, base_dv2 + tt + 1)
                    s.activation(c(b, 5), c(b, 4), AF.Ln, bias=eps_col[:, 0:1])
                    s.activation(flushc[:, 2:3], eps_col[:, 0:1], AF.Copy)
                    s.activation(c(b, 6), c(b, 5), AF.Exp, scale=-0.5)
                    s.activation(flushc[:, 2:3], c(b, 6), AF.Copy
                                 ).then_inc(sm["ac2"].h, 1)
                    s.wait_ge(sm["dv3"].h, base_dv3 + tt + 1)
                    if tt >= 4:
                        s.wait_ge(sm["tpd"].h, base_tpd + 16 * (tt // 4))
                    s.activation(flushc[:, 2:3], eps_col[:, 0:1], AF.Copy)
                    s.activation(flushc[:, 2:3], eps_col[:, 0:1], AF.Copy)
                    s.activation(xn_tm_v(tt % 4)[:, :], xt_v(tt % 2)[:, :], AF.Identity,
                                 bias=c(b, 7), scale=c(b, 8)
                                 ).then_inc(sm["xnrdy"].h, 1)

            @blk.tensor
            def _(t):
                for gidx in range(4):
                    for e in range(ECH):
                        k = gidx * ECH + e
                        t.wait_ge(sm["xnrdy"].h, base_xnrdy + 4 * (gidx + 1))
                        if k >= 2:
                            t.wait_ge(sm["tpe"].h, base_tpe + k - 1)
                        for i in range(4):
                            ins = t.transpose(bank_bf(k % 2, i * 128, (i + 1) * 128),
                                              xn_tm_v(i)[:, e * 128:(e + 1) * 128],
                                              ident_bf[:, :])
                        ins.then_inc(sm["tpd"].h, 1)

        sm["ldx"].v += 16 * 16
        sm["ac1"].v += 16
        sm["dv2"].v += 16
        sm["ac2"].v += 16
        sm["dv3"].v += 16
        sm["xnrdy"].v += 16
        sm["tpd"].v += 64
        sm["tpe"].v += 64
        if dbg == "xn" and ch == 0:
            _dump_bf16(lambda i: xn_own[:, i, :], 16, 1024)
        if dbg == "xnh" and ch == 0:
            _dump_bf16(lambda i: xn_halo_e(i)[:, :], 16, 1024)
        if dbg == "xtm" and ch == 0:
            _dump_bf16(lambda i: att[:, i * 1024:(i + 1) * 1024], 8, 1024)
        if dbg == "stats" and ch == 0:
            _dump_bf16(lambda i: stats[:, 0:16], 1, 16)

        base_wkv = sm["wkv"].v
        base_kmm, base_kev = sm["kmm"].v, sm["kev"].v
        wk_sb = att[:, 0:8192]
        wv_sb = att[:, 8192:16384]

        with _MaybeBlock(nc, phase_on()) as blk:
            @blk.gpsimd
            def _(g):
                for half in range(2):
                    g.dma_start(
                        wk_sb[:, half * 4096:(half + 1) * 4096
                              ].rearrange("p (c d) -> p c d", c=8),
                        wk.ap()[half * 1024:(half + 1) * 1024, :
                                ].rearrange("(c p) d -> p c d", p=128)
                    ).then_inc(sm["wkv"].h, 16)
                    g.dma_start(
                        wv_sb[:, half * 4096:(half + 1) * 4096
                              ].rearrange("p (c d) -> p c d", c=8),
                        wv.ap()[half * 1024:(half + 1) * 1024, :
                                ].rearrange("(c p) d -> p c d", p=128)
                    ).then_inc(sm["wkv"].h, 16)

            @blk.tensor
            def _(t):
                t.wait_ge(sm["wkv"].h, base_wkv + 64)
                grp = 0
                for dch in range(4):
                    for sch in range(4):
                        bk = grp % 4
                        if grp >= 4:
                            t.wait_ge(sm["kev"].h, base_kev + grp - 3)
                        for e in range(ECH):
                            ins = t.matmul(bank(bk),
                                           wk_sb[:, e * 512 + dch * 128: e * 512 + (dch + 1) * 128],
                                           xn_all_e(e, sch * 512, (sch + 1) * 512),
                                           start=(e == 0), stop=(e == ECH - 1))
                        ins.then_inc(sm["kmm"].h, 1)
                        grp += 1
                for st in range(16):
                    bk = grp % 4
                    t.wait_ge(sm["kev"].h, base_kev + grp - 3)
                    for e in range(ECH):
                        ins = t.matmul(bank(bk),
                                       xn_all_e(e, st * 128, (st + 1) * 128),
                                       wv_sb[:, e * 512:(e + 1) * 512],
                                       start=(e == 0), stop=(e == ECH - 1))
                    ins.then_inc(sm["kmm"].h, 1)
                    grp += 1

            @blk.scalar
            def _(s):
                grp = 0
                for dch in range(4):
                    for sch in range(4):
                        s.wait_ge(sm["kmm"].h, base_kmm + grp + 1)
                        s.activation(k_e(dch)[:, sch * 512:(sch + 1) * 512],
                                     bank(grp % 4), AF.Copy
                                     ).then_inc(sm["kev"].h, 1)
                        grp += 1
                for st in range(16):
                    s.wait_ge(sm["kmm"].h, base_kmm + grp + 1)
                    s.activation(v_st(st)[:, :], bank(grp % 4), AF.Copy
                                 ).then_inc(sm["kev"].h, 1)
                    grp += 1

        sm["wkv"].v += 64
        sm["kmm"].v += 32
        sm["kev"].v += 32
        if dbg == "k" and ch == 0:
            _dump_bf16(lambda i: k_e(i // 4)[:, (i % 4) * 512:(i % 4) * 512 + 512], 16, 512)
        if dbg == "v" and ch == 0:
            _dump_bf16(lambda i: v_st(i)[:, :], 16, 512)

        base_wql = sm["wql"].v
        base_qmm, base_qev = sm["qmm"].v, sm["qev"].v
        base_scd, base_msk = sm["scd"].v, sm["msk"].v
        base_exd, base_pnd = sm["exd"].v, sm["pnd"].v
        base_ptd, base_pte = sm["ptd"].v, sm["pte"].v
        base_pvd, base_atv = sm["pvd"].v, sm["atv"].v

        SB_EVEN = (0, 1, 2)
        SB_ODD = (4, 5, 6)

        def sc_banks(qt):
            return SB_EVEN if qt % 2 == 0 else SB_ODD

        def sc_ap(qt, lo, hi):
            bks = sc_banks(qt)
            assert hi - lo <= 512
            b0 = bks[lo // 512]
            if (hi - 1) // 512 == lo // 512:
                return bank(b0, lo % 512, (hi - 1) % 512 + 1)
            raise AssertionError("cross-bank slice")

        with _MaybeBlock(nc, phase_on()) as blk:
            @blk.gpsimd
            def _(g):
                for h in range(HEADS):
                    if h >= 2:
                        g.wait_ge(sm["qmm"].h, base_qmm + 2 * (h - 1))
                    src = wq.ap()[:, h * 128:(h + 1) * 128
                                  ].rearrange("(c p) d -> p c d", p=128)
                    dst = wq_v(h % 2).rearrange("p (c d) -> p c d", c=16)
                    g.dma_start(dst[:, 0:8, :], src[:, 0:8, :]).then_inc(sm["wql"].h, 16)
                    g.dma_start(dst[:, 8:16, :], src[:, 8:16, :]).then_inc(sm["wql"].h, 16)

            @blk.tensor
            def _(t):
                for h in range(HEADS):
                    gg = h // 4
                    t.wait_ge(sm["wql"].h, base_wql + 32 * (h + 1))
                    if h > 0:
                        t.wait_ge(sm["exd"].h, base_exd + 8 * h)
                        t.wait_ge(sm["atv"].h, base_atv + 8 * h)
                        t.wait_ge(sm["qev"].h, base_qev + 2 * h)
                    for tc in range(2):
                        for e in range(ECH):
                            ins = t.matmul(bank(6 + tc),
                                           wq_v(h % 2)[:, e * 128:(e + 1) * 128],
                                           xn_own[:, e, tc * 512:(tc + 1) * 512],
                                           start=(e == 0), stop=(e == ECH - 1))
                        ins.then_inc(sm["qmm"].h, 1)
                    for qt in range(8):
                        idx = (h * 8 + qt)
                        bks = sc_banks(qt)
                        t.wait_ge(sm["qev"].h, base_qev + 2 * (h + 1))
                        if idx >= 2:
                            t.wait_ge(sm["exd"].h, base_exd + idx - 1)
                        s0 = qt * 128
                        for part in range(3):
                            lo = part * 512
                            hi = min(lo + 512, 1152)
                            need_hb = (ch == 0 and qt < 8 and
                                       s0 + lo < 1024 and part < 2)
                            ins = t.matmul(bank(bks[part], 0, hi - lo),
                                           q_v()[:, qt * 128:(qt + 1) * 128],
                                           k_e(gg)[:, s0 + lo: s0 + hi],
                                           start=True, stop=not need_hb)
                            if need_hb:
                                hb_hi = min(hi, 1024 - s0)
                                ins = t.matmul(bank(bks[part], 0, hb_hi - lo),
                                               onerow[0:1, :],
                                               hbias_sb[0:1, s0 + lo: s0 + hb_hi],
                                               start=False, stop=True)
                        ins.then_inc(sm["scd"].h, 1)
                        t.wait_ge(sm["pnd"].h, base_pnd + idx + 1)
                        if idx >= 1:
                            t.wait_ge(sm["pte"].h, base_pte + 2 * idx)
                            t.wait_ge(sm["atv"].h, base_atv + idx)
                        for i in range(8):
                            t.transpose(bank_bf(3, i * 128, (i + 1) * 128),
                                        p_v(idx % 2)[:, i * 128:(i + 1) * 128],
                                        ident_bf[:, :])
                        t.transpose(bank_bf(7, 0, 128),
                                    p_v(idx % 2)[:, 1024:1152], ident_bf[:, :]
                                    ).then_inc(sm["ptd"].h, 1)
                        t.wait_ge(sm["pte"].h, base_pte + 2 * (idx + 1))
                        for i in range(9):
                            ins = t.matmul(bank(7, 64, 192),
                                           v_st(qt + i)[:, gg * 128:(gg + 1) * 128],
                                           pT_v()[:, i * 128:(i + 1) * 128],
                                           start=(i == 0), stop=(i == 8))
                        ins.then_inc(sm["pvd"].h, 1)

            @blk.vector
            def _(v):
                for h in range(HEADS):
                    for qt in range(8):
                        idx = h * 8 + qt
                        b = idx % 2
                        bks = sc_banks(qt)
                        nm3 = smv[:, 8 * b: 8 * b + 3]
                        nm = smv[:, 8 * b + 3: 8 * b + 4]
                        xb = smv[:, 8 * b + 4: 8 * b + 5]
                        ssum = smv[:, 8 * b + 5: 8 * b + 6]
                        rcp = smv[:, 8 * b + 6: 8 * b + 7]
                        v.wait_ge(sm["scd"].h, base_scd + idx + 1)
                        v.tensor_tensor(bank(bks[0], 0, 128), bank(bks[0], 0, 128),
                                        trim0_sb[:, :], ALU.add)
                        v.tensor_tensor(bank(bks[2], 0, 128), bank(bks[2], 0, 128),
                                        trim8_sb[:, :], ALU.add)
                        sp = lambda: v.tensor_copy(flushc[:, 3:4], eps_col[:, 0:1])
                        for part in range(3):
                            hi = 512 if part < 2 else 128
                            v.reduce_max(nm3[:, part:part + 1],
                                         bank(bks[part], 0, hi), AX.X, negate=True)
                        sp()
                        v.tensor_reduce(nm, nm3, AX.X, ALU.min)
                        sp()
                        v.tensor_scalar(xb, nm, SCALE, None, ALU.mult)
                        v.tensor_copy(flushc[:, b:b + 1], xb)
                        v.tensor_copy(flushw[:, 0:128], trim0_sb[:, :]
                                      ).then_inc(sm["msk"].h, 1)
                        v.wait_ge(sm["exd"].h, base_exd + idx + 1)
                        v.reduce_sum(ssum, asum[:, 4 * b:4 * b + 3], AX.X)
                        sp()
                        v.reciprocal(rcp, ssum)
                        sp()
                        v.tensor_scalar(p_v(b)[:, :], p_v(b)[:, :], rcp, None,
                                        ALU.mult)
                        v.tensor_copy(flushb[:, :], p_v(b)[:, 1024:1152]
                                      ).then_inc(sm["pnd"].h, 1)

            @blk.scalar
            def _(s):
                for h in range(HEADS):
                    for tc in range(2):
                        s.wait_ge(sm["qmm"].h, base_qmm + 2 * h + tc + 1)
                        s.activation(q_v()[:, tc * 512:(tc + 1) * 512],
                                     bank(6 + tc), AF.Copy)
                        s.activation(flushc[:, 2:3],
                                     q_v()[:, tc * 512 + 511:tc * 512 + 512].bitcast(F32)[:, 0:1]
                                     if False else eps_col[:, 0:1],
                                     AF.Copy).then_inc(sm["qev"].h, 1)
                    for qt in range(8):
                        idx = h * 8 + qt
                        b = idx % 2
                        bks = sc_banks(qt)
                        xb = smv[:, 8 * b + 4: 8 * b + 5]
                        s.wait_ge(sm["msk"].h, base_msk + idx + 1)
                        if idx >= 2:
                            s.wait_ge(sm["ptd"].h, base_ptd + idx - 1)
                        for part in range(3):
                            lo = part * 512
                            hi = min(lo + 512, 1152)
                            s.activation(p_v(b)[:, lo:hi],
                                         bank(bks[part], 0, hi - lo), AF.Exp,
                                         bias=xb, scale=SCALE,
                                         accum_out=asum[:, 4 * b + part:4 * b + part + 1])
                        s.activation(flushc[:, b + 2:b + 3], asum[:, 4 * b + 2:4 * b + 3],
                                     AF.Copy).then_inc(sm["exd"].h, 1)
                        s.wait_ge(sm["ptd"].h, base_ptd + idx + 1)
                        s.activation(pT_v()[:, 0:1024], bank_bf(3, 0, 1024), AF.Copy)
                        s.activation(pT_v()[:, 1024:1152], bank_bf(7, 0, 128), AF.Copy)
                        s.activation(flushc[:, 2:3], eps_col[:, 0:1], AF.Copy)
                        s.activation(flushc[:, 2:3], pT_v()[:, 1151:1152], AF.Copy
                                     ).then_inc(sm["pte"].h, 2)
                        s.wait_ge(sm["pvd"].h, base_pvd + idx + 1)
                        s.activation(att_v(h)[:, qt * 128:(qt + 1) * 128],
                                     bank(7, 64, 192), AF.Copy
                                     ).then_inc(sm["atv"].h, 1)

        sm["wql"].v += 32 * HEADS
        sm["qmm"].v += 2 * HEADS
        sm["qev"].v += 2 * HEADS
        for nm2 in ["scd", "msk", "exd", "pnd", "ptd", "pvd", "atv"]:
            sm[nm2].v += 128
        sm["pte"].v += 256
        if dbg == "att" and ch == 0:
            _dump_bf16(lambda i: att_v(i)[:, :], 16, 1024)
        if dbg == "q" and ch == 0:
            _dump_bf16(lambda i: q_v()[:, :], 1, 1024)

        base_wol, base_c4m, base_c4e = sm["wol"].v, sm["c4m"].v, sm["c4e"].v

        with _MaybeBlock(nc, phase_on()) as blk:
            @blk.gpsimd
            def _(g):
                n = 0
                for m2g in range(4):
                    for mch in range(ECH):
                        if n >= 2:
                            g.wait_ge(sm["c4m"].h, base_c4m + n - 1)
                        g.dma_start(wo_v(n % 2)[:, :],
                                    wo.ap()[mch * 128:(mch + 1) * 128,
                                            m2g * 512:(m2g + 1) * 512]
                                    ).then_inc(sm["wol"].h, 16)
                        n += 1

            @blk.tensor
            def _(t):
                n = 0
                for m2g in range(4):
                    if m2g > 0:
                        t.wait_ge(sm["c4e"].h, base_c4e + 8 * m2g)
                    for mch in range(ECH):
                        t.wait_ge(sm["wol"].h, base_wol + 16 * (n + 1))
                        for m2c in range(4):
                            for tc in range(2):
                                ins = t.matmul(bank(m2c * 2 + tc),
                                               wo_v(n % 2)[:, m2c * 128:(m2c + 1) * 128],
                                               att_v(mch)[:, tc * 512:(tc + 1) * 512],
                                               start=(mch == 0), stop=(mch == ECH - 1))
                        ins.then_inc(sm["c4m"].h, 1)
                        n += 1

            @blk.scalar
            def _(s):
                for m2g in range(4):
                    s.wait_ge(sm["c4m"].h, base_c4m + 16 * (m2g + 1))
                    for m2c in range(4):
                        m2 = m2g * 4 + m2c
                        for tc in range(2):
                            s.activation(oa_v(m2)[:, tc * 512:(tc + 1) * 512],
                                         bank(m2c * 2 + tc), AF.Identity,
                                         bias=bo_sb[:, m2:m2 + 1]
                                         ).then_inc(sm["c4e"].h, 1)

        sm["wol"].v += 16 * 64
        sm["c4m"].v += 64
        sm["c4e"].v += 32
        if dbg == "oa" and ch == 0:
            with nc.Block() as blk:
                @blk.sync
                def _(sync):
                    for i in range(16):
                        sync.dma_start(out.ap()[i * 128:(i + 1) * 128, 0:1024],
                                       oa_v(i)[:, :]).then_inc(sm["sto"].h, 16)
                    sync.wait_ge(sm["sto"].h, sm["sto"].v + 16 * 16)
            sm["sto"].v += 16 * 16

        base_csd = sm["csd"].v
        with _MaybeBlock(nc, phase_on()) as blk:
            @blk.tensor
            def _(t):
                for tc in range(2):
                    for e in range(ECH):
                        ins = t.matmul(pb[tc][0:1, :],
                                       onecol[:, 0:1],
                                       xn_own[:, e, tc * 512:(tc + 1) * 512],
                                       start=(e == 0), stop=(e == ECH - 1))
                ins.then_inc(sm["csd"].h, 1)

            @blk.scalar
            def _(s):
                s.wait_ge(sm["csd"].h, base_csd + 1)
                s.activation(colsum_bf[0:1, 0:512], pb[0][0:1, :], AF.Copy)
                s.activation(colsum_bf[0:1, 512:1024], pb[1][0:1, :], AF.Copy)
        sm["csd"].v += 1

        for piece in range(NPIECE):
            j0 = piece * JH
            base_gul, base_c3m = sm["gul"].v, sm["c3m"].v
            base_sil, base_up, base_c3h = sm["c3sil"].v, sm["c3up"].v, sm["c3h"].v

            with _MaybeBlock(nc, phase_on()) as blk:
                @blk.gpsimd
                def _(g):
                    for jj in range(JH):
                        j = j0 + jj
                        if jj >= 2:
                            g.wait_ge(sm["c3m"].h, base_c3m + 4 * (jj - 1))
                        g.dma_start(wg_v(jj % 2).rearrange("p (c d) -> p c d", c=16),
                                    g_blk.ap()[j].rearrange("c p d -> p c d")
                                    ).then_inc(sm["gul"].h, 16)
                        g.dma_start(wu_v(jj % 2).rearrange("p (c d) -> p c d", c=16),
                                    u_blk.ap()[j].rearrange("c p d -> p c d")
                                    ).then_inc(sm["gul"].h, 16)

                @blk.tensor
                def _(t):
                    for jj in range(JH):
                        j = j0 + jj
                        t.wait_ge(sm["gul"].h, base_gul + 32 * (jj + 1))
                        bb = 4 * (jj % 2)
                        if jj >= 2:
                            t.wait_ge(sm["c3h"].h, base_c3h + 2 * (jj - 1))
                        for tc in range(2):
                            for e in range(ECH):
                                t.matmul(bank(bb + tc),
                                         wg_v(jj % 2)[:, e * 128:(e + 1) * 128],
                                         xn_own[:, e, tc * 512:(tc + 1) * 512],
                                         start=(e == 0), stop=False)
                            t.matmul(bank(bb + tc), neg128row[0:1, :],
                                     colsum_bf[0:1, tc * 512:(tc + 1) * 512],
                                     start=False, stop=True)
                        for tc in range(2):
                            for e in range(ECH):
                                t.matmul(bank(bb + 2 + tc),
                                         wu_v(jj % 2)[:, e * 128:(e + 1) * 128],
                                         xn_own[:, e, tc * 512:(tc + 1) * 512],
                                         start=(e == 0), stop=False)
                            ins = t.matmul(bank(bb + 2 + tc), neg128row[0:1, :],
                                           colsum_bf[0:1, tc * 512:(tc + 1) * 512],
                                           start=False, stop=True)
                        ins.then_inc(sm["c3m"].h, 4)

                @blk.scalar
                def _(s):
                    for jj in range(JH):
                        j = j0 + jj
                        bb = 4 * (jj % 2)
                        s.wait_ge(sm["c3m"].h, base_c3m + 4 * (jj + 1))
                        if jj >= 1:
                            s.wait_ge(sm["c3h"].h, base_c3h + 2 * jj)
                        for tc in range(2):
                            s.activation(g_v()[:, tc * 512:(tc + 1) * 512],
                                         bank(bb + tc), AF.Silu,
                                         scale=sg_sb[:, j:j + 1])
                        s.activation(flushc[:, 2:3], g_v()[:, 1023:1024].bitcast(F32)[:, 0:1]
                                     if False else eps_col[:, 0:1], AF.Copy
                                     ).then_inc(sm["c3sil"].h, 2)

                @blk.vector
                def _(v):
                    for jj in range(JH):
                        j = j0 + jj
                        bb = 4 * (jj % 2)
                        v.wait_ge(sm["c3m"].h, base_c3m + 4 * (jj + 1))
                        for tc in range(2):
                            v.tensor_scalar(u_v()[:, tc * 512:(tc + 1) * 512],
                                            bank(bb + 2 + tc),
                                            su_sb[:, j:j + 1], None, ALU.mult)
                        v.wait_ge(sm["c3sil"].h, base_sil + 2 * (jj + 1))
                        v.tensor_tensor(h_v(jj)[:, 0:512], g_v()[:, 0:512],
                                        u_v()[:, 0:512], ALU.mult)
                        v.tensor_tensor(h_v(jj)[:, 512:1024], g_v()[:, 512:1024],
                                        u_v()[:, 512:1024], ALU.mult
                                        ).then_inc(sm["c3h"].h, 2)

            sm["gul"].v += 32 * JH
            sm["c3m"].v += 4 * JH
            sm["c3sil"].v += 2 * JH
            sm["c3h"].v += 2 * JH

            base_hsm, base_hse, base_hsd = sm["hsm"].v, sm["hse"].v, sm["hsd"].v
            with _MaybeBlock(nc, phase_on()) as blk:
                @blk.tensor
                def _(t):
                    for tc in range(2):
                        for jj in range(JH):
                            ins = t.matmul(pb[tc][0:1, :],
                                           onecol[:, 0:1],
                                           h_v(jj)[:, tc * 512:(tc + 1) * 512],
                                           start=(jj == 0), stop=(jj == JH - 1))
                    ins.then_inc(sm["hsm"].h, 1)

                @blk.scalar
                def _(s):
                    s.wait_ge(sm["hsm"].h, base_hsm + 1)
                    s.activation(hsum32[0:1, 0:512], pb[0][0:1, :], AF.Copy)
                    s.activation(hsum32[0:1, 512:1024], pb[1][0:1, :], AF.Copy
                                 ).then_inc(sm["hse"].h, 1)

                @blk.vector
                def _(v):
                    v.wait_ge(sm["hse"].h, base_hse + 1)
                    v.tensor_copy(hsum_hi[0:1, :], hsum32[0:1, :])
                    v.tensor_copy(flushc[:, 3:4], eps_col[:, 0:1])
                    v.tensor_copy(flushc[:, 3:4], eps_col[:, 0:1])
                    v.scalar_tensor_tensor(hsum_lo[0:1, :], hsum_hi[0:1, :], -1.0,
                                           hsum32[0:1, :], ALU.mult, ALU.add)
            sm["hsm"].v += 1
            sm["hse"].v += 1

            base_wdl, base_c5m, base_c5e = sm["wdl"].v, sm["c5m"].v, sm["c5e"].v
            with _MaybeBlock(nc, phase_on()) as blk:
                @blk.gpsimd
                def _(g):
                    def grp_end(n):
                        return 23 * (n // 2) + 11 * (n % 2) + 11
                    n = 0
                    for m2g in range(4):
                        for jgl in range(2):
                            jg = piece * 2 + jgl
                            if n >= 2:
                                g.wait_ge(sm["c5m"].h, base_c5m + grp_end(n - 2))
                            g.dma_start(
                                wd_v(n % 2).rearrange("p (c d) -> p c d", c=11),
                                d_blk.ap()[m2g, jg]).then_inc(sm["wdl"].h, 16)
                            n += 1

                @blk.tensor
                def _(t):
                    n = 0
                    for m2g in range(4):
                        if m2g > 0:
                            t.wait_ge(sm["c5e"].h, base_c5e + 8 * m2g)
                        for jgl in range(2):
                            t.wait_ge(sm["wdl"].h, base_wdl + 16 * (n + 1))
                            for js in range(11):
                                jj = jgl * 11 + js
                                for m2c in range(4):
                                    for tc in range(2):
                                        ins = t.matmul(
                                            bank(m2c * 2 + tc),
                                            wd_v(n % 2)[:, js * 512 + m2c * 128:
                                                        js * 512 + (m2c + 1) * 128],
                                            h_v(jj)[:, tc * 512:(tc + 1) * 512],
                                            start=(jj == 0), stop=False)
                                ins.then_inc(sm["c5m"].h, 1)
                            n += 1
                        for m2c in range(4):
                            for tc in range(2):
                                t.matmul(bank(m2c * 2 + tc), neg128row[0:1, :],
                                         hsum_hi[0:1, tc * 512:(tc + 1) * 512],
                                         start=False, stop=False)
                                ins = t.matmul(bank(m2c * 2 + tc), neg128row[0:1, :],
                                               hsum_lo[0:1, tc * 512:(tc + 1) * 512],
                                               start=False, stop=True)
                        ins.then_inc(sm["c5m"].h, 1)

                @blk.vector
                def _(v):
                    for m2g in range(4):
                        v.wait_ge(sm["c5m"].h, base_c5m + 23 * (m2g + 1))
                        for m2c in range(4):
                            m2 = m2g * 4 + m2c
                            for tc in range(2):
                                oav = oa_v(m2)[:, tc * 512:(tc + 1) * 512]
                                v.scalar_tensor_tensor(
                                    oav, bank(m2c * 2 + tc), sd_sb[:, m2:m2 + 1],
                                    oav, ALU.mult, ALU.add
                                ).then_inc(sm["c5e"].h, 1)

            sm["wdl"].v += 16 * 8
            sm["c5m"].v += 23 * 4
            sm["c5e"].v += 32

        base_ldr, base_tp6 = sm["ldr"].v, sm["tp6"].v
        base_ev6, base_sto = sm["ev6"].v, sm["sto"].v
        with _MaybeBlock(nc, phase_on()) as blk:
            @blk.sync
            def _(sync):
                for tt in range(8):
                    if tt >= 1:
                        sync.wait_ge(sm["ev6"].h, base_ev6 + 4 * tt)
                    r0 = 1024 + ch * 1024 + tt * 128
                    sync.dma_start(xres_v()[:, :],
                                   xloc.ap()[r0:r0 + 128, :]
                                   ).then_inc(sm["ldr"].h, 16)
                    sync.wait_ge(sm["sto"].h, base_sto + 16 * tt)

            @blk.tensor
            def _(t):
                for tt in range(8):
                    if tt >= 1:
                        t.wait_ge(sm["ev6"].h, base_ev6 + 4 * tt)
                    for m2 in range(16):
                        ins = t.transpose(bank(m2 // 4, (m2 % 4) * 128, (m2 % 4 + 1) * 128),
                                          oa_v(m2)[:, tt * 128:(tt + 1) * 128],
                                          ident_f[:, :])
                    ins.then_inc(sm["tp6"].h, 1)

            @blk.vector
            def _(v):
                for tt in range(8):
                    v.wait_ge(sm["tp6"].h, base_tp6 + tt + 1)
                    v.wait_ge(sm["ldr"].h, base_ldr + 16 * (tt + 1))
                    if tt >= 1:
                        v.wait_ge(sm["sto"].h, base_sto + 16 * tt)
                    for b in range(4):
                        v.tensor_tensor(osb_v()[:, b * 512:(b + 1) * 512],
                                        bank(b), xres_v()[:, b * 512:(b + 1) * 512],
                                        ALU.add).then_inc(sm["ev6"].h, 1)

            @blk.scalar
            def _(s):
                for tt in range(8):
                    s.wait_ge(sm["ev6"].h, base_ev6 + 4 * (tt + 1))
                    s.dma_start(out.ap()[ch * 1024 + tt * 128:
                                         ch * 1024 + (tt + 1) * 128, :],
                                osb_v()[:, :]).then_inc(sm["sto"].h, 16)

        sm["ldr"].v += 16 * 8
        sm["tp6"].v += 8
        sm["ev6"].v += 32
        sm["sto"].v += 16 * 8

    with _MaybeBlock(nc, phase_on()) as blk:
        @blk.sync
        def _(sync):
            sync.wait_ge(sm["sto"].h, sm["sto"].v)

    return nc


_CACHE = {}


def kernel(L0_attn_x, L0_attn_Wq, L0_attn_Wk, L0_attn_Wv, L0_attn_Wo, L0_attn_bo,
           L0_ffn_wq_gate, L0_ffn_s_gate, L0_ffn_wq_up, L0_ffn_s_up,
           L0_ffn_wq_down, L0_ffn_s_down):
    x = np.asarray(L0_attn_x, dtype=np.float32)
    B, T, _ = x.shape
    assert (B, T) == (2, 8192)

    def u8(a):
        a = np.asarray(a, dtype=np.int32)
        return np.ascontiguousarray(a.view(np.uint8).reshape(a.shape + (4,))[..., 0])

    wg8 = u8(L0_ffn_wq_gate)
    wu8 = u8(L0_ffn_wq_up)
    wd8 = u8(L0_ffn_wq_down)

    g_blk = np.ascontiguousarray(
        wg8.reshape(NJ, 128, ECH, 128).transpose(0, 2, 3, 1))
    u_blk = np.ascontiguousarray(
        wu8.reshape(NJ, 128, ECH, 128).transpose(0, 2, 3, 1))
    d_blk = np.ascontiguousarray(
        wd8.reshape(4, 512, NJ, 128).transpose(0, 2, 3, 1)
           .reshape(4, 4, 11, 128, 512).transpose(0, 1, 3, 2, 4))

    sg_col = np.ascontiguousarray(np.asarray(L0_ffn_s_gate, np.float32).reshape(NJ, 128).T)
    su_col = np.ascontiguousarray(np.asarray(L0_ffn_s_up, np.float32).reshape(NJ, 128).T)
    sd_col = np.ascontiguousarray(np.asarray(L0_ffn_s_down, np.float32).reshape(16, 128).T)
    bo_col = np.ascontiguousarray(np.asarray(L0_attn_bo, np.float32).reshape(16, 128).T)

    ii = np.arange(128)[:, None]
    jj = np.arange(128)[None, :]
    trim0 = np.where(jj > ii, 0.0, NEG).astype(np.float32)
    trim8 = np.where(jj <= ii, 0.0, NEG).astype(np.float32)

    wq32 = np.ascontiguousarray(np.asarray(L0_attn_Wq, np.float32))
    wk32 = np.ascontiguousarray(np.asarray(L0_attn_Wk, np.float32))
    wv32 = np.ascontiguousarray(np.asarray(L0_attn_Wv, np.float32))
    wo32 = np.ascontiguousarray(np.asarray(L0_attn_Wo, np.float32))

    hb_on = np.full(1024, NEG, np.float32).astype(ml_dtypes.bfloat16)
    hb_off = np.zeros(1024, np.float32).astype(ml_dtypes.bfloat16)

    if "nc" not in _CACHE:
        _CACHE["nc"] = build_nc()
    nc = _CACHE["nc"]

    xpad = np.zeros((2, 1024 + T, E), np.float32)
    xpad[:, 1024:] = x
    xflat = np.ascontiguousarray(xpad.reshape(2 * (1024 + T), E))

    in_maps = []
    for core in range(8):
        b = core // 4
        s0 = (core % 4) * OWN
        in_maps.append({
            "x3": xflat, "wq": wq32, "wk": wk32, "wv": wv32, "wo": wo32,
            "bo_col": bo_col, "g_blk": g_blk, "u_blk": u_blk, "d_blk": d_blk,
            "sg_col": sg_col, "su_col": su_col, "sd_col": sd_col,
            "trim0": trim0, "trim8": trim8,
            "hbias": hb_on if s0 == 0 else hb_off,
        })

    res = run_bass_kernel_spmd(nc, in_maps, core_ids=list(range(8)))
    outs = [r["out"] for r in res.results]
    full = np.empty((2, 8192, E), np.float32)
    for core in range(8):
        b = core // 4
        s0 = (core % 4) * OWN
        full[b, s0:s0 + OWN] = outs[core]
    return full
